# version 12
# baseline (speedup 1.0000x reference)
import numpy as np
import ml_dtypes
from contextlib import ExitStack

import concourse.bass as bass
import concourse.mybir as mybir
from concourse.bass_utils import run_bass_kernel_spmd
from concourse.alu_op_type import AluOpType as op

F32 = mybir.dt.float32
BF16 = mybir.dt.bfloat16
I32 = mybir.dt.int32
AF = mybir.ActivationFunctionType
AX = mybir.AxisListType.X

P = 128
D = 1024
F = 4096
NT = 16
TB = 8
TLOC = NT * P
NCORE = 8
EPS = 1e-5
GAMMA = 32.0
INV_D = 1.0 / D
INV_NW = 1.0 / (4096 * 1024)
INV127 = 1.0 / 127.0


class Sem:
    __slots__ = ("h", "n", "name")


class B:
    def __init__(self, nc, stack):
        self.nc = nc
        self.stack = stack
        self.plan = {"sync": [], "tensor": [], "vector": [], "scalar": [], "gpsimd": []}

    def sem(self, name):
        s = Sem()
        s.h = self.stack.enter_context(self.nc.semaphore(name))
        s.n = 0
        s.name = name
        return s

    def w(self, *sems):
        return tuple((s.h, s.n) for s in sems if s.n > 0)

    def emit(self, eng, fn, waits=(), inc=None, inc_n=1):
        if inc is not None:
            inc.n += inc_n
        ih = inc.h if inc is not None else None

        def run(e, fn=fn, waits=waits, ih=ih, inc_n=inc_n):
            for (h, n) in waits:
                e.wait_ge(h, n)
            r = fn(e)
            if ih is not None:
                r.then_inc(ih, inc_n)
        self.plan[eng].append(run)

    def wait(self, eng, *sems):
        waits = self.w(*sems)

        def run(e, waits=waits):
            for (h, n) in waits:
                e.wait_ge(h, n)
        self.plan[eng].append(run)

    def nop_inc(self, eng, sem, k=1):
        sem.n += k

        def run(e, h=sem.h, k=k):
            e.nop().then_inc(h, k)
        self.plan[eng].append(run)

    def dma(self, qeng, out, in_, sem, waits=(), transpose=False):
        prior = sem.n
        sem.n += 16

        def run(e, out=out, in_=in_, waits=waits, h=sem.h, t=transpose, prior=prior):
            if prior > 0:
                e.wait_ge(h, prior)
            for (hh, n) in waits:
                e.wait_ge(hh, n)
            e.dma_start(out=out, in_=in_, transpose=t).then_inc(h, 16)
        self.plan[qeng].append(run)


def _r3(ap2d, inner):
    return ap2d.rearrange("p (a i) -> p a i", i=inner)


def build_nc():
    nc = bass.Bass("TRN2", target_bir_lowering=False, detect_race_conditions=False)

    x_h = nc.dram_tensor("x", [TLOC, D], F32, kind="ExternalInput").ap()
    wq_h = nc.dram_tensor("wq", [D, D], F32, kind="ExternalInput").ap()
    wk_h = nc.dram_tensor("wk", [D, D], F32, kind="ExternalInput").ap()
    wv_h = nc.dram_tensor("wv", [D, D], F32, kind="ExternalInput").ap()
    wg_h = nc.dram_tensor("wg", [F, D], F32, kind="ExternalInput").ap()
    wu_h = nc.dram_tensor("wu", [F, D], F32, kind="ExternalInput").ap()
    wd_h = nc.dram_tensor("wd", [D, F], F32, kind="ExternalInput").ap()
    wgsh_h = nc.dram_tensor("wgsh", [512, D], F32, kind="ExternalInput").ap()
    wush_h = nc.dram_tensor("wush", [512, D], F32, kind="ExternalInput").ap()
    wdsh_h = nc.dram_tensor("wdsh", [128, F], F32, kind="ExternalInput").ap()
    out_h = nc.dram_tensor("out", [TLOC, D], F32, kind="ExternalOutput").ap()

    wsum_in = nc.dram_tensor("wsum_in", [1, 8], F32, kind="Internal").ap()
    wsum_out = nc.dram_tensor("wsum_out", [1, 8], F32, kind="Internal").ap()
    kvin = [nc.dram_tensor(f"kvin{b}", [D, D], F32, kind="Internal").ap() for b in range(2)]
    kvout = [nc.dram_tensor(f"kvout{b}", [D, D], F32, kind="Internal").ap() for b in range(2)]
    x1sp = nc.dram_tensor("x1sp", [TLOC, D], F32, kind="Internal").ap()
    cqg = nc.dram_tensor("cqg", [8 * D, 512], BF16, kind="Internal").ap()
    cqu = nc.dram_tensor("cqu", [8 * D, 512], BF16, kind="Internal").ap()
    cqd = nc.dram_tensor("cqd", [F, D], BF16, kind="Internal").ap()

    est = ExitStack()
    with est:
        ones_c = est.enter_context(nc.sbuf_tensor("ones_c", [P, 1], F32))
        ones_r = est.enter_context(nc.sbuf_tensor("ones_r", [1, P], F32))
        wstats = est.enter_context(nc.sbuf_tensor("wstats", [P, 8], F32))
        T1 = est.enter_context(nc.sbuf_tensor("T1", [P, 16384], BF16, is_transpose_dest=True))
        wsums = est.enter_context(nc.sbuf_tensor("wsums", [P, 4], F32))
        wsum_sb = est.enter_context(nc.sbuf_tensor("wsum_sb", [1, 8], F32))
        wpack = est.enter_context(nc.sbuf_tensor("wpack", [1, 8], F32))
        eps_c = est.enter_context(nc.sbuf_tensor("eps_c", [P, 1], F32))
        junkV = est.enter_context(nc.sbuf_tensor("junkV", [P, 1], F32))
        junkS = est.enter_context(nc.sbuf_tensor("junkS", [P, 1], F32))
        junkG = est.enter_context(nc.sbuf_tensor("junkG", [P, 1], F32))
        names = ("a_sum a_sq a_r a_nb y_sum y_sq y_r y_nb b_sum b_sq b_r b_nb "
                 "d_sum d_sq d_r d_nb clipx sx dscg dscu cliph sh dscd tmp1 tmp2 mx").split()
        st = {n: est.enter_context(nc.sbuf_tensor(n, [P, NT], F32)) for n in names}

        psum = [est.enter_context(nc.psum_tensor(f"ps{i}", [P, 512], F32)) for i in range(8)]

        bld = B(nc, est)
        ld_x = bld.sem("ld_x"); ld_x2 = bld.sem("ld_x2"); ld_sh = bld.sem("ld_sh")
        ld_wk = bld.sem("ld_wk"); ld_wv = bld.sem("ld_wv"); ld_wq = bld.sem("ld_wq")
        ld_p = [bld.sem(f"ld_p{i}") for i in range(4)]
        pq = bld.sem("pq")
        ld_cg = [bld.sem(f"ld_cg{i}") for i in range(2)]
        ld_cu = [bld.sem(f"ld_cu{i}") for i in range(2)]
        ld_dj = [bld.sem(f"ld_dj{i}") for i in range(4)]
        ld_kv = bld.sem("ld_kv"); ld_x1r = bld.sem("ld_x1r"); ld_ws = bld.sem("ld_ws")
        st_kv = bld.sem("st_kv"); st_x1 = bld.sem("st_x1"); st_ws = bld.sem("st_ws")
        st_cd = bld.sem("st_cd"); st_cg = bld.sem("st_cg"); st_cu = bld.sem("st_cu")
        st_out = bld.sem("st_out")
        tr_ln = bld.sem("tr_ln"); tr_xq = bld.sem("tr_xq"); tr_hq = bld.sem("tr_hq")
        tr_p = bld.sem("tr_p")
        cc_ws = bld.sem("cc_ws"); cc_kv0 = bld.sem("cc_kv0"); cc_kv1 = bld.sem("cc_kv1")
        wsr = bld.sem("wsr")
        vch = bld.sem("vch"); sch = bld.sem("sch"); gch = bld.sem("gch")
        kvrdy = bld.sem("kvrdy")
        t_ms = bld.sem("t_ms")
        ph = bld.sem("ph")
        full = [bld.sem(f"full{i}") for i in range(8)]
        free = [bld.sem(f"free{i}") for i in range(8)]

        V, S, G, T, Y = "vector", "scalar", "gpsimd", "tensor", "sync"

        def ve(fn, waits=(), inc=None):
            bld.emit(V, fn, waits=waits, inc=(inc if inc is not None else vch))

        def se(fn, waits=(), inc=None):
            bld.emit(S, fn, waits=waits, inc=(inc if inc is not None else sch))

        def ge(fn, waits=(), inc=None):
            bld.emit(G, fn, waits=waits, inc=(inc if inc is not None else gch))

        def mm_group(bank, pairs, waits=(), out=None):
            o = out if out is not None else psum[bank][:]
            wl = bld.w(free[bank]) + tuple(waits)
            n = len(pairs)
            full[bank].n += 1

            def run(e, o=o, pairs=pairs, wl=wl, n=n, fh=full[bank].h):
                for (h, c) in wl:
                    e.wait_ge(h, c)
                for i, (l, r) in enumerate(pairs):
                    r_ = e.matmul(o, l, r, start=(i == 0), stop=(i == n - 1))
                    if i == n - 1:
                        r_.then_inc(fh, 1)
            bld.plan[T].append(run)

        att = ExitStack()
        qT = att.enter_context(nc.sbuf_tensor("qT", [P, 16384], BF16))
        wbfA = att.enter_context(nc.sbuf_tensor("wbfA", [P, 8192], BF16))
        wbfQ = att.enter_context(nc.sbuf_tensor("wbfQ", [P, 8192], BF16))
        wbfB = qT[:, 0:8192]
        kvev = att.enter_context(nc.sbuf_tensor("kvev", [P, 2048], F32))
        kvf32 = att.enter_context(nc.sbuf_tensor("kvf32", [P, 2048], F32))
        kvbf = wbfA[:, 0:8192]
        xld = att.enter_context(nc.sbuf_tensor("xld", [P, 4 * D], F32))
        lnbf = att.enter_context(nc.sbuf_tensor("lnbf", [P, D], BF16))
        x1t = att.enter_context(nc.sbuf_tensor("x1t", [P, D], F32))
        xqt = att.enter_context(nc.sbuf_tensor("xqt", [P, D], BF16))
        pstage = att.enter_context(nc.sbuf_tensor("pstage", [P, 4096], F32))
        pi32 = att.enter_context(nc.sbuf_tensor("pi32", [P, 2048], I32))
        pqbf = att.enter_context(nc.sbuf_tensor("pqbf", [P, 2048], BF16))
        ptrp = att.enter_context(nc.sbuf_tensor("ptrp", [P, 4096], BF16,
                                                is_transpose_dest=True))
        att2 = ExitStack()
        kbuf = att2.enter_context(nc.sbuf_tensor("kbuf", [P, 8 * D], BF16))
        vbuf = att2.enter_context(nc.sbuf_tensor("vbuf", [P, 8 * D], BF16))

        ve(lambda e: e.memset(ones_c[:], 1.0))
        ve(lambda e: e.memset(wsum_sb[:], 0.0))
        ve(lambda e: e.memset(ones_r[:], 1.0))
        ve(lambda e: e.memset(eps_c[:], EPS))

        def run_cc(e, i, o):
            return e.collective_compute(
                "AllReduce", op.add, replica_groups=[list(range(NCORE))],
                ins=[i], outs=[o])

        def ln_chain(sum_b, sq_b, r_b, nb_b, c0, c1):
            sl = slice(c0, c1)
            ve(lambda e: e.tensor_scalar(st["tmp1"][:, sl], sum_b[:, sl], INV_D, None,
                                         op.mult), waits=bld.w(vch, sch))
            bld.wait(V, vch)
            ve(lambda e: e.tensor_tensor(st["tmp2"][:, sl], st["tmp1"][:, sl],
                                         st["tmp1"][:, sl], op.mult))
            bld.wait(V, vch)
            ve(lambda e: e.scalar_tensor_tensor(st["tmp2"][:, sl], sq_b[:, sl], INV_D,
                                                st["tmp2"][:, sl], op.mult, op.subtract))
            se(lambda e: e.activation(r_b[:, sl], st["tmp2"][:, sl], AF.Sqrt,
                                      bias=eps_c[:, 0:1]), waits=bld.w(vch))
            ve(lambda e: e.reciprocal(r_b[:, sl], r_b[:, sl]), waits=bld.w(sch))
            bld.wait(V, vch)
            ve(lambda e: e.scalar_tensor_tensor(nb_b[:, sl], st["tmp1"][:, sl], -1.0,
                                                r_b[:, sl], op.mult, op.mult))

        def loadw_bf(w_h, dst_bf, ld_sem, waits=()):
            srcw = w_h.rearrange("(kb p) d -> p kb d", p=P)
            bld.dma(G, _r3(dst_bf, D), srcw, ld_sem, waits=waits)

        loadw_bf(wk_h, wbfA[:], ld_wk)
        loadw_bf(wv_h, wbfB, ld_wv)
        loadw_bf(wq_h, wbfQ[:], ld_wq)

        for grp in range(4):
            tiles = range(grp * 4, (grp + 1) * 4)
            src = x_h[grp * 512:(grp + 1) * 512, :].rearrange("(a p) d -> p a d", p=P)
            bld.dma(G, _r3(xld[:], D), src, ld_x, waits=bld.w(vch, sch) if grp else ())
            wl = ((ld_x.h, ld_x.n),)
            for i, t in enumerate(tiles):
                ve(lambda e, t=t, i=i: e.tensor_reduce(
                    st["a_sum"][:, t:t + 1], xld[:, i * D:(i + 1) * D], AX, op.add),
                    waits=wl)
                se(lambda e, t=t, i=i: e.activation(
                    x1t[:], xld[:, i * D:(i + 1) * D], AF.Square,
                    accum_out=st["a_sq"][:, t:t + 1]), waits=wl + bld.w(sch))
            ln_chain(st["a_sum"], st["a_sq"], st["a_r"], st["a_nb"], grp * 4, grp * 4 + 4)
            for i, t in enumerate(tiles):
                ve(lambda e, t=t, i=i: e.tensor_scalar(
                    lnbf[:], xld[:, i * D:(i + 1) * D],
                    st["a_r"][:, t:t + 1], st["a_nb"][:, t:t + 1], op.mult, op.add),
                    waits=bld.w(vch) + bld.w(tr_ln))
                dst = _r3(T1[:], 2048)[:, :, t * P:(t + 1) * P]
                srcv = _r3(lnbf[:], P)
                bld.dma(Y, dst, srcv, tr_ln, waits=bld.w(vch), transpose=True)

        for m in range(3):
            for hh in range(2):
                if m < 2:
                    sh_h = (wgsh_h, wush_h)[m]
                    ssrc = sh_h[hh * 256:(hh + 1) * 256, :].rearrange(
                        "(ob p) i -> p ob i", p=P)
                    sdst = _r3(kvf32[:], D)
                else:
                    ssrc = wdsh_h[:, hh * 2048:(hh + 1) * 2048]
                    sdst = kvf32[:]
                bld.dma(G, sdst, ssrc, ld_sh, waits=bld.w(wsr))
                dstc = wsums[:, m:m + 1] if hh == 0 else st["tmp1"][:, 0:1]
                ve(lambda e, dstc=dstc: e.tensor_reduce(
                    dstc, kvf32[:], AX, op.add, apply_absolute_value=True),
                    waits=bld.w(ld_sh), inc=wsr)
            ve(lambda e, m=m: e.tensor_tensor(
                wsums[:, m:m + 1], wsums[:, m:m + 1], st["tmp1"][:, 0:1],
                op.add), waits=bld.w(wsr), inc=wsr)

        def wsum_b1():
            mm_group(7, [(ones_c[:], wsums[:, 0:3])], waits=bld.w(wsr),
                     out=psum[7][0:1, 0:3])
            ve(lambda e: e.tensor_copy(wsum_sb[0:1, 0:3], psum[7][0:1, 0:3]),
               waits=bld.w(full[7]), inc=free[7])
            bld.dma(G, wsum_in, wsum_sb[:], st_ws, waits=bld.w(free[7]))
            bld.emit(G, lambda e: run_cc(e, wsum_in, wsum_out),
                     waits=bld.w(st_ws), inc=cc_ws)
            bld.dma(G, wpack[:], wsum_out, ld_ws, waits=bld.w(cc_ws))
            ve(lambda e: e.tensor_scalar(wpack[0:1, 0:3], wpack[0:1, 0:3], INV_NW,
                                         1e-5, op.mult, op.max),
               waits=bld.w(ld_ws), inc=wsr)
            ve(lambda e: e.tensor_scalar(wpack[0:1, 3:6], wpack[0:1, 0:3], INV127,
                                         None, op.mult), waits=bld.w(wsr), inc=wsr)
            ve(lambda e: e.reciprocal(wpack[0:1, 0:3], wpack[0:1, 0:3]),
               waits=bld.w(wsr), inc=wsr)

        def wsum_b2():
            mm_group(7, [(ones_r[:], wpack[0:1, 0:8])], waits=bld.w(wsr),
                     out=psum[7][:, 0:8])
            ve(lambda e: e.tensor_copy(wstats[:], psum[7][:, 0:8]),
               waits=bld.w(full[7]), inc=free[7])
            ws_rdy[0] = bld.w(free[7])

        ws_rdy = [()]

        psub = [0]
        last_pst = [()]

        def _prep_sub(load_src, scol, dst, store_fn=None):
            n = psub[0]
            psub[0] += 1
            slot = n % 4
            bld.dma(G, pstage[:, slot * 1024:(slot + 1) * 1024], load_src,
                    ld_p[slot], waits=((pq.h, n - 3),) if n >= 4 else ())
            ve(lambda e, slot=slot: e.tensor_scalar(
                pi32[:, 0:1024], pstage[:, slot * 1024:(slot + 1) * 1024],
                wstats[:, scol:scol + 1], None, op.mult),
                waits=bld.w(ld_p[slot]) + tuple(ws_rdy[0]), inc=pq)
            qs = n % 2
            bld.wait(V, pq)
            ve(lambda e, qs=qs: e.tensor_scalar(
                pqbf[:, qs * 1024:(qs + 1) * 1024], pi32[:, 0:1024], 1.0, -1.0,
                op.min, op.max), waits=((tr_p.h, max(0, 16 * (psub[0] - 2))),))
            srcv = _r3(pqbf[:, qs * 1024:(qs + 1) * 1024], P)
            bld.dma(Y, dst, srcv, tr_p,
                    waits=bld.w(vch) + tuple(last_pst[0]), transpose=True)
            if store_fn is not None:
                store_fn()

        def _mk_store(cache_slice, src_view, st_c):
            def store():
                bld.dma(Y, cache_slice, src_view, st_c, waits=bld.w(tr_p))
                last_pst[0] = ((st_c.h, st_c.n),)
            return store

        prep_units = []
        for c in range(8):
            for sub in range(4):
                load_src = wd_h[c * P:(c + 1) * P, sub * 1024:(sub + 1) * 1024]
                dst = _r3(ptrp[:], P)[:, sub * 8:(sub + 1) * 8, :]
                sf = None
                if sub == 3:
                    csl = cqd.rearrange("(jb p) d -> p jb d", p=P)[:, :, c * P:(c + 1) * P]
                    sf = _mk_store(csl, _r3(ptrp[:], P), st_cd)
                prep_units.append((load_src, 2, dst, sf))
        for oc in range(8):
            for m_h, scol, cache_h, st_c in ((wg_h, 0, cqg, st_cg),
                                             (wu_h, 1, cqu, st_cu)):
                for ob in range(4):
                    load_src = m_h[oc * 512 + ob * P: oc * 512 + (ob + 1) * P, :]
                    dst = _r3(ptrp[:], 512)[:, :, ob * P:(ob + 1) * P]
                    sf = None
                    if ob == 3:
                        csl = cache_h[oc * D:(oc + 1) * D, :].rearrange(
                            "(ib p) o -> p ib o", p=P)
                        sf = _mk_store(csl, _r3(ptrp[:], 512), st_c)
                    prep_units.append((load_src, scol, dst, sf))

        prep_i = [0]

        def emit_prep(n=1):
            if not ws_rdy[0]:
                return
            for _ in range(n):
                if prep_i[0] < len(prep_units):
                    prep_units[prep_i[0]]
                    _prep_sub(*prep_units[prep_i[0]])
                    prep_i[0] += 1



        def kv_proj(b, wbf, obuf, wsem):
            gi = 0
            for tb in range(TB):
                tglob = b * TB + tb
                for oc in range(2):
                    bank = gi % 4
                    gi += 1
                    pairs = []
                    for kb in range(8):
                        l = T1[:, kb * 2048 + tglob * P: kb * 2048 + (tglob + 1) * P]
                        r = wbf[:, kb * D + oc * 512: kb * D + (oc + 1) * 512]
                        pairs.append((l, r))
                    mm_group(bank, pairs,
                             waits=((tr_ln.h, 16 * (tglob + 1)),) + bld.w(wsem))
                    se(lambda e, bank=bank, obuf=obuf, tb=tb, oc=oc: e.copy(
                        obuf[:, tb * D + oc * 512: tb * D + (oc + 1) * 512],
                        psum[bank][:]), waits=bld.w(full[bank]), inc=free[bank])
                    emit_prep(1)

        def kv_partial(b):
            for jc in range(2):
                for g in range(2):
                    banks = [4 * g + ii for ii in range(4)]
                    for ii in range(4):
                        ib = g * 4 + ii
                        pairs = []
                        for tb in range(TB):
                            l = kbuf[:, tb * D + ib * P: tb * D + (ib + 1) * P]
                            r = vbuf[:, tb * D + jc * 512: tb * D + (jc + 1) * 512]
                            pairs.append((l, r))
                        mm_group(banks[ii], pairs, waits=bld.w(*free[0:4]))
                    for ii in range(4):
                        bank = banks[ii]
                        ve(lambda e, bank=bank, ii=ii: e.tensor_copy(
                            kvev[:, ii * 512:(ii + 1) * 512], psum[bank][:]),
                            waits=bld.w(full[bank]) + bld.w(st_kv), inc=free[bank])
                    dst = kvin[b].rearrange("(ib p) j -> p ib j", p=P)[
                        :, g * 4:(g + 1) * 4, jc * 512:(jc + 1) * 512]
                    bld.dma(Y, dst, _r3(kvev[:], 512), st_kv,
                            waits=bld.w(*free[4 * g:4 * g + 4]))
                    emit_prep(1)

        kv_proj(0, wbfA[:], kbuf[:], ld_wk)
        wsum_b1()
        kv_proj(0, wbfB, vbuf[:], ld_wv)
        wsum_b2()
        kv_partial(0)
        bld.emit(G, lambda e: run_cc(e, kvin[0], kvout[0]), waits=bld.w(st_kv),
                 inc=cc_kv0)

        kv_proj(1, wbfA[:], kbuf[:], ld_wk)
        kv_proj(1, wbfB, vbuf[:], ld_wv)
        kv_partial(1)
        bld.emit(G, lambda e: run_cc(e, kvin[1], kvout[1]), waits=bld.w(st_kv),
                 inc=cc_kv1)

        bld.nop_inc(T, t_ms, 1)
        ms_kv_done = t_ms.n
        for dqb in range(8):
            for tc in range(4):
                bank = (dqb * 4 + tc) % 4
                pairs = []
                for kb in range(8):
                    l = wbfQ[:, kb * D + dqb * P: kb * D + (dqb + 1) * P]
                    r = T1[:, kb * 2048 + tc * 512: kb * 2048 + (tc + 1) * 512]
                    pairs.append((l, r))
                mm_group(bank, pairs, waits=bld.w(ld_wq))
                se(lambda e, bank=bank, dqb=dqb, tc=tc: e.copy(
                    qT[:, dqb * 2048 + tc * 512: dqb * 2048 + (tc + 1) * 512],
                    psum[bank][:]), waits=bld.w(full[bank]), inc=free[bank])
                emit_prep(1)
        bld.nop_inc(T, t_ms, 1)
        ms_q_done = t_ms.n

        emit_prep(99)

        att2.close()
        ybf = att.enter_context(nc.sbuf_tensor("ybf", [P, NT * D], BF16))

        for b in range(2):
            ccb = cc_kv0 if b == 0 else cc_kv1
            for h in range(4):
                src = kvout[b][h * 256:(h + 1) * 256, :].rearrange("(ib p) j -> p ib j",
                                                                   p=P)
                bld.dma(S, _r3(kvf32[:], D), src, ld_kv,
                        waits=((ccb.h, 1),) + bld.w(kvrdy) +
                              (bld.w(vch) if (b == 0 and h == 0) else ()))
                wkv = bld.w(ld_kv) + ((t_ms.h, (ms_kv_done if b == 0 else t_ms.n)),)
                ve(lambda e, h=h: e.tensor_scalar(
                    kvbf[:, h * 2048:(h + 1) * 2048], kvf32[:], GAMMA, None, op.mult),
                    waits=wkv, inc=kvrdy)

            for tb in range(TB):
                t = b * TB + tb
                for jc in range(2):
                    bank = (tb * 2 + jc) % 4
                    pairs = []
                    for dmb in range(8):
                        l = qT[:, dmb * 2048 + t * P: dmb * 2048 + (t + 1) * P]
                        r = kvbf[:, dmb * D + jc * 512: dmb * D + (jc + 1) * 512]
                        pairs.append((l, r))
                    mm_group(bank, pairs, waits=bld.w(kvrdy))
                    ve(lambda e, bank=bank, t=t, jc=jc: e.tensor_copy(
                        ybf[:, t * D + jc * 512: t * D + (jc + 1) * 512],
                        psum[bank][:]), waits=bld.w(full[bank]), inc=free[bank])
            bld.nop_inc(T, t_ms, 1)

        x1s = [kvf32[:, 0:1024], kvev[:, 0:1024]]
        ln2s = [kvf32[:, 1024:2048], kvev[:, 1024:2048]]
        qis = [_r3(pi32[:], 1024)[:, 0, :], _r3(pi32[:], 1024)[:, 1, :]]
        KVEV_FREE = bld.w(st_kv)
        KVF_FREE = bld.w(kvrdy)
        for grp in range(4):
            tiles = list(range(grp * 4, (grp + 1) * 4))
            for i, t in enumerate(tiles):
                wly = bld.w(*free[0:4])
                ve(lambda e, t=t: e.tensor_reduce(
                    st["y_sum"][:, t:t + 1], ybf[:, t * D:(t + 1) * D], AX, op.add),
                    waits=wly)
                se(lambda e, t=t: e.activation(
                    x1t[:], ybf[:, t * D:(t + 1) * D], AF.Square,
                    accum_out=st["y_sq"][:, t:t + 1]), waits=wly + bld.w(sch))
            ln_chain(st["y_sum"], st["y_sq"], st["y_r"], st["y_nb"], grp * 4, grp * 4 + 4)
            src = x_h[grp * 512:(grp + 1) * 512, :].rearrange("(a p) d -> p a d", p=P)
            bld.dma(S, _r3(xld[:], D), src, ld_x2, waits=bld.w(vch))
            for pr in range(2):
                pts = tiles[pr * 2:(pr + 1) * 2]
                for s, t in enumerate(pts):
                    i = pr * 2 + s
                    bld.wait(V, vch)
                    ve(lambda e, t=t, s=s: e.tensor_scalar(
                        x1s[s], ybf[:, t * D:(t + 1) * D], st["y_r"][:, t:t + 1],
                        st["y_nb"][:, t:t + 1], op.mult, op.add),
                        waits=bld.w(st_x1) +
                              (KVEV_FREE if s else KVF_FREE))
                    bld.wait(V, vch)
                    ve(lambda e, s=s, i=i: e.tensor_tensor(
                        x1s[s], x1s[s], xld[:, i * D:(i + 1) * D], op.add),
                        waits=bld.w(ld_x2))
                for s, t in enumerate(pts):
                    bld.dma(Y, x1sp[t * P:(t + 1) * P, :], x1s[s], st_x1,
                            waits=bld.w(vch))
                for s, t in enumerate(pts):
                    bld.wait(V, vch)
                    ve(lambda e, t=t, s=s: e.tensor_reduce(
                        st["b_sum"][:, t:t + 1], x1s[s], AX, op.add))
                    se(lambda e, t=t, s=s: e.activation(
                        ybf[:, t * D:(t + 1) * D], x1s[s], AF.Square,
                        accum_out=st["b_sq"][:, t:t + 1]), waits=bld.w(vch, sch))
                ln_chain(st["b_sum"], st["b_sq"], st["b_r"], st["b_nb"],
                         pts[0], pts[0] + 2)
                for s, t in enumerate(pts):
                    bld.wait(V, vch)
                    ve(lambda e, t=t, s=s: e.tensor_scalar(
                        ln2s[s], x1s[s], st["b_r"][:, t:t + 1],
                        st["b_nb"][:, t:t + 1], op.mult, op.add),
                        waits=bld.w(sch))
                for s, t in enumerate(pts):
                    bld.wait(V, vch)
                    ve(lambda e, t=t, s=s: e.tensor_reduce(
                        st["mx"][:, t:t + 1], ln2s[s], AX, op.max,
                        apply_absolute_value=True))
                bld.wait(V, vch)
                ve(lambda e, t0=pts[0]: e.tensor_scalar(
                    st["clipx"][:, t0:t0 + 2], st["mx"][:, t0:t0 + 2], 1e-5, None,
                    op.max))
                bld.wait(V, vch)
                ve(lambda e, t0=pts[0]: e.reciprocal(st["tmp2"][:, 0:2],
                                                     st["clipx"][:, t0:t0 + 2]))
                bld.wait(V, vch)
                ve(lambda e, t0=pts[0]: e.tensor_scalar(
                    st["sx"][:, t0:t0 + 2], st["tmp2"][:, 0:2], 127.0, None, op.mult))
                for s, t in enumerate(pts):
                    bld.wait(V, vch)
                    ve(lambda e, t=t, s=s: e.tensor_scalar(
                        qis[s], ln2s[s], st["sx"][:, t:t + 1], None, op.mult))
                    se(lambda e, s=s: e.copy(xqt[:], qis[s]),
                        waits=bld.w(vch) + bld.w(tr_xq))
                    dst = _r3(T1[:], 2048)[:, :, t * P:(t + 1) * P]
                    srcv = _r3(xqt[:], P)
                    bld.dma(Y, dst, srcv, tr_xq,
                            waits=bld.w(sch) + ((t_ms.h, ms_q_done),), transpose=True)

        ve(lambda e: e.tensor_scalar(st["dscg"][:], st["clipx"][:], wstats[:, 3:4],
                                     None, op.mult), waits=bld.w(vch))
        bld.wait(V, vch)
        ve(lambda e: e.tensor_scalar(st["dscu"][:], st["clipx"][:], wstats[:, 4:5],
                                     None, op.mult))

        bld.wait(Y, tr_ln, st_kv, st_x1, tr_xq, tr_p, st_cd, st_cg, st_cu)
        bld.wait(G, ld_x, ld_sh, ld_wk, ld_wv, ld_wq, ld_ws, *ld_p)
        bld.emit(V, lambda e: e.memset(junkV[:], 0.0), inc=ph)
        bld.emit(S, lambda e: e.memzero(junkS[:]), inc=ph)
        bld.emit(G, lambda e: e.memset(junkG[:], 0.0), inc=ph)
        bld.nop_inc(T, ph, 1)
        bld.nop_inc(Y, ph, 1)
        att.close()

        ffn = ExitStack()
        hbuf = ffn.enter_context(nc.sbuf_tensor("hbuf", [P, 8 * F], BF16))
        hqTg = ffn.enter_context(nc.sbuf_tensor("hqTg", [P, 32 * 512], BF16,
                                                is_transpose_dest=True))
        wqtg = ffn.enter_context(nc.sbuf_tensor("wqtg", [P, 2 * 4096], BF16,
                                                is_transpose_dest=True))
        wqtu = ffn.enter_context(nc.sbuf_tensor("wqtu", [P, 2 * 4096], BF16,
                                                is_transpose_dest=True))
        wdqb = ffn.enter_context(nc.sbuf_tensor("wdqb", [P, 4 * D], BF16))
        hqi32 = ffn.enter_context(nc.sbuf_tensor("hqi32", [P, 2048], I32))
        hqbf = ffn.enter_context(nc.sbuf_tensor("hqbf", [P, F], BF16,
                                                is_transpose_dest=True))
        gsil = ffn.enter_context(nc.sbuf_tensor("gsil", [P, 2 * 512], F32))
        dnf = ffn.enter_context(nc.sbuf_tensor("dnf", [P, D], F32))
        x1r = ffn.enter_context(nc.sbuf_tensor("x1r", [P, D], F32))
        ftrash = ffn.enter_context(nc.sbuf_tensor("ftrash", [P, D], F32))

        for eng in (V, S, G, T, Y):
            bld.wait(eng, ph)

        def cache_load(cache_h, sems, dstbuf, oc, extra=()):
            slot = oc % 2
            src = cache_h[oc * D:(oc + 1) * D, :].rearrange("(ib p) o -> p ib o", p=P)
            bld.dma(Y, _r3(dstbuf[:, slot * 4096:(slot + 1) * 4096], 512), src,
                    sems[slot], waits=tuple(extra))

        h_rdy = {}
        for half in range(2):
            tiles = list(range(half * TB, (half + 1) * TB))
            for oc in range(8):
                if half == 0 and oc < 2:
                    cache_load(cqg, ld_cg, wqtg[:], oc, extra=bld.w(st_cg))
                    cache_load(cqu, ld_cu, wqtu[:], oc, extra=bld.w(st_cu))
                slot = oc % 2
                wts = ((ld_cg[slot].h, ld_cg[slot].n), (ld_cu[slot].h, ld_cu[slot].n),
                       (tr_xq.h, 16 * NT))
                for ti, t in enumerate(tiles):
                    gbank = 2 * (ti % 2)
                    ubank = gbank + 1
                    gp, up = [], []
                    for ib in range(8):
                        l = T1[:, ib * 2048 + t * P: ib * 2048 + (t + 1) * P]
                        gp.append((l, wqtg[:, slot * 4096 + ib * 512:
                                            slot * 4096 + (ib + 1) * 512]))
                        up.append((l, wqtu[:, slot * 4096 + ib * 512:
                                            slot * 4096 + (ib + 1) * 512]))
                    mm_group(gbank, gp, waits=wts)
                    mm_group(ubank, up)
                    gslot = ti % 2
                    se(lambda e, gbank=gbank, t=t, s=gslot: e.activation(
                        gsil[:, s * 512:(s + 1) * 512], psum[gbank][:], AF.Silu,
                        scale=st["dscg"][:, t:t + 1]),
                        waits=bld.w(full[gbank]) + bld.w(*free[0:4]),
                        inc=free[gbank])
                    hcol = (t % TB) * F + oc * 512
                    ve(lambda e, ubank=ubank, t=t, s=gslot, hcol=hcol:
                        e.scalar_tensor_tensor(
                            hbuf[:, hcol:hcol + 512], psum[ubank][:],
                            st["dscu"][:, t:t + 1], gsil[:, s * 512:(s + 1) * 512],
                            op.mult, op.mult),
                        waits=bld.w(full[ubank], free[gbank]), inc=free[ubank])
                    if oc == 7:
                        h_rdy[t] = (free[ubank].h, free[ubank].n)
                bld.nop_inc(T, t_ms, 1)
                nh, noc = (half, oc + 2) if oc + 2 < 8 else (half + 1, oc + 2 - 8)
                if nh < 2:
                    gate_w = ((t_ms.h, t_ms.n),)
                    cache_load(cqg, ld_cg, wqtg[:], noc, extra=gate_w)
                    cache_load(cqu, ld_cu, wqtu[:], noc, extra=gate_w)

            for grp in range(2):
                gtiles = tiles[grp * 4:(grp + 1) * 4]
                for gi, t in enumerate(gtiles):
                    tl = t % TB
                    bld.wait(V, vch)
                    ve(lambda e, tl=tl: e.tensor_reduce(
                        st["mx"][:, 0:1], hbuf[:, tl * F:(tl + 1) * F], AX, op.max,
                        apply_absolute_value=True), waits=(h_rdy[t],))
                    bld.wait(V, vch)
                    ve(lambda e, t=t: e.tensor_scalar(st["cliph"][:, t:t + 1],
                                                      st["mx"][:, 0:1], 1e-5, None,
                                                      op.max))
                    bld.wait(V, vch)
                    ve(lambda e, t=t: e.reciprocal(st["tmp2"][:, 0:1],
                                                   st["cliph"][:, t:t + 1]))
                    bld.wait(V, vch)
                    ve(lambda e, t=t: e.tensor_scalar(st["sh"][:, t:t + 1],
                                                      st["tmp2"][:, 0:1], 127.0,
                                                      None, op.mult))
                    bld.wait(V, vch)
                    ve(lambda e, t=t: e.tensor_scalar(st["dscd"][:, t:t + 1],
                                                      st["cliph"][:, t:t + 1],
                                                      wstats[:, 5:6], None, op.mult))
                    for hh in range(2):
                        bld.wait(V, vch, sch)
                        ve(lambda e, tl=tl, t=t, hh=hh: e.tensor_scalar(
                            hqi32[:],
                            hbuf[:, tl * F + hh * 2048: tl * F + (hh + 1) * 2048],
                            st["sh"][:, t:t + 1], None, op.mult))
                        se(lambda e, hh=hh: e.copy(
                            hqbf[:, hh * 2048:(hh + 1) * 2048], hqi32[:]),
                           waits=bld.w(vch) + bld.w(tr_hq) + bld.w(st_cd))
                    dst = _r3(hqTg[:], 512)[:, :, gi * P:(gi + 1) * P]
                    bld.dma(Y, dst, _r3(hqbf[:], P), tr_hq,
                            waits=bld.w(sch) + bld.w(t_ms), transpose=True)
                for bank in range(8):
                    bld.wait(T, free[bank])
                t_ms_base = t_ms.n
                for jb in range(32):
                    slot = jb % 4
                    ldw = (((t_ms.h, t_ms_base + jb - 3),) if jb >= 4 else bld.w(st_cd))
                    bld.dma(Y, wdqb[:, slot * D:(slot + 1) * D],
                            cqd[jb * P:(jb + 1) * P, :], ld_dj[slot], waits=ldw)
                    wl = ((ld_dj[slot].h, ld_dj[slot].n), (tr_hq.h, tr_hq.n))
                    last = (jb == 31)
                    if not last:
                        t_ms.n += 1
                    else:
                        for bank in range(8):
                            full[bank].n += 1

                    def runj(e, jb=jb, slot=slot, wl=wl, last=last,
                             tmh=t_ms.h, fhs=[full[i].h for i in range(8)]):
                        for (h, n) in wl:
                            e.wait_ge(h, n)
                        k = 0
                        for tg in range(4):
                            for oc2 in range(2):
                                r_ = e.matmul(
                                    psum[tg * 2 + oc2][:],
                                    hqTg[:, jb * 512 + tg * P: jb * 512 + (tg + 1) * P],
                                    wdqb[:, slot * D + oc2 * 512:
                                         slot * D + (oc2 + 1) * 512],
                                    start=(jb == 0), stop=last)
                                if last:
                                    r_.then_inc(fhs[tg * 2 + oc2], 1)
                                elif k == 7:
                                    r_.then_inc(tmh, 1)
                                k += 1
                    bld.plan[T].append(runj)
                for gi, t in enumerate(gtiles):
                    se(lambda e, gi=gi, t=t: e.activation(
                        dnf[:, 0:512], psum[gi * 2][:], AF.Copy,
                        scale=st["dscd"][:, t:t + 1]),
                        waits=bld.w(full[gi * 2], vch) + bld.w(st_out),
                        inc=free[gi * 2])
                    se(lambda e, gi=gi, t=t: e.activation(
                        dnf[:, 512:1024], psum[gi * 2 + 1][:], AF.Copy,
                        scale=st["dscd"][:, t:t + 1]),
                        waits=bld.w(full[gi * 2 + 1]), inc=free[gi * 2 + 1])
                    ve(lambda e, t=t: e.tensor_reduce(st["d_sum"][:, t:t + 1], dnf[:],
                                                      AX, op.add),
                       waits=bld.w(free[gi * 2], free[gi * 2 + 1]))
                    se(lambda e, t=t: e.activation(ftrash[:], dnf[:], AF.Square,
                                                   accum_out=st["d_sq"][:, t:t + 1]),
                       waits=bld.w(free[gi * 2], free[gi * 2 + 1], sch))
                    ln_chain(st["d_sum"], st["d_sq"], st["d_r"], st["d_nb"], t, t + 1)
                    bld.wait(V, vch)
                    ve(lambda e, t=t: e.tensor_scalar(dnf[:], dnf[:],
                                                      st["d_r"][:, t:t + 1],
                                                      st["d_nb"][:, t:t + 1],
                                                      op.mult, op.add))
                    bld.dma(G, x1r[:], x1sp[t * P:(t + 1) * P, :], ld_x1r,
                            waits=bld.w(vch) + bld.w(st_x1))
                    bld.wait(V, vch)
                    ve(lambda e, t=t: e.tensor_tensor(dnf[:], x1r[:], dnf[:], op.add),
                       waits=bld.w(ld_x1r))
                    bld.dma(Y, out_h[t * P:(t + 1) * P, :], dnf[:], st_out,
                            waits=bld.w(vch))

        bld.wait(Y, st_out)
        ffn.close()

        with nc.Block() as block:
            @block.sync
            def _(e):
                for f in bld.plan["sync"]:
                    f(e)

            @block.tensor
            def _(e):
                for f in bld.plan["tensor"]:
                    f(e)

            @block.vector
            def _(e):
                for f in bld.plan["vector"]:
                    f(e)

            @block.scalar
            def _(e):
                for f in bld.plan["scalar"]:
                    f(e)

            @block.gpsimd
            def _(e):
                for f in bld.plan["gpsimd"]:
                    f(e)

    return nc


_NC = None


def _get_nc():
    global _NC
    if _NC is None:
        _NC = build_nc()
    return _NC


def kernel(L0_attn_x, L0_attn_w_q, L0_attn_w_k, L0_attn_w_v,
           L0_ffn_w_gate, L0_ffn_w_up, L0_ffn_w_down):
    x = np.ascontiguousarray(np.asarray(L0_attn_x, dtype=np.float32))
    wq = np.ascontiguousarray(np.asarray(L0_attn_w_q, dtype=np.float32))
    wk = np.ascontiguousarray(np.asarray(L0_attn_w_k, dtype=np.float32))
    wv = np.ascontiguousarray(np.asarray(L0_attn_w_v, dtype=np.float32))
    wg = np.ascontiguousarray(np.asarray(L0_ffn_w_gate, dtype=np.float32))
    wu = np.ascontiguousarray(np.asarray(L0_ffn_w_up, dtype=np.float32))
    wd = np.ascontiguousarray(np.asarray(L0_ffn_w_down, dtype=np.float32))

    nc = _get_nc()
    in_maps = []
    for c in range(NCORE):
        sl = slice(c * 1024, (c + 1) * 1024)
        xc = np.ascontiguousarray(np.concatenate([x[0, sl], x[1, sl]], axis=0))
        in_maps.append({
            "x": xc, "wq": wq, "wk": wk, "wv": wv, "wg": wg, "wu": wu, "wd": wd,
            "wgsh": np.ascontiguousarray(wg[c * 512:(c + 1) * 512]),
            "wush": np.ascontiguousarray(wu[c * 512:(c + 1) * 512]),
            "wdsh": np.ascontiguousarray(wd[c * 128:(c + 1) * 128]),
        })
    res = run_bass_kernel_spmd(nc, in_maps, core_ids=list(range(NCORE)))
    out = np.empty((2, 8192, 1024), np.float32)
    for c in range(NCORE):
        sl = slice(c * 1024, (c + 1) * 1024)
        r = np.asarray(res.results[c]["out"], np.float32)
        out[0, sl] = r[:1024]
        out[1, sl] = r[1024:]
    return out


# version 16 (final)
# speedup vs baseline: 1.3034x; 1.3034x over previous
import numpy as np
import ml_dtypes
from contextlib import ExitStack

import concourse.bass as bass
import concourse.mybir as mybir
from concourse.bass_utils import run_bass_kernel_spmd
from concourse.alu_op_type import AluOpType as op

F32 = mybir.dt.float32
BF16 = mybir.dt.bfloat16
I32 = mybir.dt.int32
AF = mybir.ActivationFunctionType
AX = mybir.AxisListType.X

P = 128
D = 1024
F = 4096
NT = 16
TB = 8
TLOC = NT * P
NCORE = 8
EPS = 1e-5
GAMMA = 32.0
INV_D = 1.0 / D
INV_NW = 1.0 / (4096 * 1024)
INV127 = 1.0 / 127.0


class Sem:
    __slots__ = ("h", "n", "name")


class B:
    def __init__(self, nc, stack):
        self.nc = nc
        self.stack = stack
        self.plan = {"sync": [], "tensor": [], "vector": [], "scalar": [], "gpsimd": []}

    def sem(self, name):
        s = Sem()
        s.h = self.stack.enter_context(self.nc.semaphore(name))
        s.n = 0
        s.name = name
        return s

    def w(self, *sems):
        return tuple((s.h, s.n) for s in sems if s.n > 0)

    def emit(self, eng, fn, waits=(), inc=None, inc_n=1):
        if inc is not None:
            inc.n += inc_n
        ih = inc.h if inc is not None else None

        def run(e, fn=fn, waits=waits, ih=ih, inc_n=inc_n):
            for (h, n) in waits:
                e.wait_ge(h, n)
            r = fn(e)
            if ih is not None:
                r.then_inc(ih, inc_n)
        self.plan[eng].append(run)

    def wait(self, eng, *sems):
        waits = self.w(*sems)

        def run(e, waits=waits):
            for (h, n) in waits:
                e.wait_ge(h, n)
        self.plan[eng].append(run)

    def nop_inc(self, eng, sem, k=1):
        sem.n += k

        def run(e, h=sem.h, k=k):
            e.nop().then_inc(h, k)
        self.plan[eng].append(run)

    def dma(self, qeng, out, in_, sem, waits=(), transpose=False):
        prior = sem.n
        sem.n += 16

        def run(e, out=out, in_=in_, waits=waits, h=sem.h, t=transpose, prior=prior):
            if prior > 0:
                e.wait_ge(h, prior)
            for (hh, n) in waits:
                e.wait_ge(hh, n)
            e.dma_start(out=out, in_=in_, transpose=t).then_inc(h, 16)
        self.plan[qeng].append(run)


def _r3(ap2d, inner):
    return ap2d.rearrange("p (a i) -> p a i", i=inner)


def build_nc():
    nc = bass.Bass("TRN2", target_bir_lowering=False, detect_race_conditions=False)

    x_h = nc.dram_tensor("x", [TLOC, D], F32, kind="ExternalInput").ap()
    wq_h = nc.dram_tensor("wq", [D, D], F32, kind="ExternalInput").ap()
    wk_h = nc.dram_tensor("wk", [D, D], F32, kind="ExternalInput").ap()
    wv_h = nc.dram_tensor("wv", [D, D], F32, kind="ExternalInput").ap()
    wg_h = nc.dram_tensor("wg", [F, D], F32, kind="ExternalInput").ap()
    wu_h = nc.dram_tensor("wu", [F, D], F32, kind="ExternalInput").ap()
    wd_h = nc.dram_tensor("wd", [D, F], F32, kind="ExternalInput").ap()
    wgsh_h = nc.dram_tensor("wgsh", [512, D], F32, kind="ExternalInput").ap()
    wush_h = nc.dram_tensor("wush", [512, D], F32, kind="ExternalInput").ap()
    wdsh_h = nc.dram_tensor("wdsh", [D, 512], F32, kind="ExternalInput").ap()
    out_h = nc.dram_tensor("out", [TLOC, D], F32, kind="ExternalOutput").ap()

    wsum_in = nc.dram_tensor("wsum_in", [1, 8], F32, kind="Internal").ap()
    wsum_out = nc.dram_tensor("wsum_out", [1, 8], F32, kind="Internal").ap()
    kvin = [nc.dram_tensor(f"kvin{b}", [D, D], F32, kind="Internal").ap() for b in range(2)]
    kvout = [nc.dram_tensor(f"kvout{b}", [D, D], F32, kind="Internal").ap() for b in range(2)]
    x1sp = nc.dram_tensor("x1sp", [TLOC, D], F32, kind="Internal").ap()
    cqg = nc.dram_tensor("cqg", [8 * D, 512], BF16, kind="Internal").ap()
    cqu = nc.dram_tensor("cqu", [8 * D, 512], BF16, kind="Internal").ap()
    cqd = nc.dram_tensor("cqd", [F, D], BF16, kind="Internal").ap()
    cqg_in = nc.dram_tensor("cqg_in", [D, 512], BF16, kind="Internal").ap()
    cqu_in = nc.dram_tensor("cqu_in", [D, 512], BF16, kind="Internal").ap()
    cqd_in = nc.dram_tensor("cqd_in", [512, D], BF16, kind="Internal").ap()

    est = ExitStack()
    with est:
        ones_c = est.enter_context(nc.sbuf_tensor("ones_c", [P, 1], F32))
        ones_r = est.enter_context(nc.sbuf_tensor("ones_r", [1, P], F32))
        wstats = est.enter_context(nc.sbuf_tensor("wstats", [P, 8], F32))
        T1 = est.enter_context(nc.sbuf_tensor("T1", [P, 16384], BF16, is_transpose_dest=True))
        wsums = est.enter_context(nc.sbuf_tensor("wsums", [P, 4], F32))
        wsum_sb = est.enter_context(nc.sbuf_tensor("wsum_sb", [1, 8], F32))
        wpack = est.enter_context(nc.sbuf_tensor("wpack", [1, 8], F32))
        eps_c = est.enter_context(nc.sbuf_tensor("eps_c", [P, 1], F32))
        junkV = est.enter_context(nc.sbuf_tensor("junkV", [P, 1], F32))
        junkS = est.enter_context(nc.sbuf_tensor("junkS", [P, 1], F32))
        junkG = est.enter_context(nc.sbuf_tensor("junkG", [P, 1], F32))
        names = ("a_sum a_sq a_r a_nb y_sum y_sq y_r y_nb b_sum b_sq b_r b_nb "
                 "d_sum d_sq d_r d_nb clipx sx dscg dscu cliph sh dscd tmp1 tmp2 mx").split()
        st = {n: est.enter_context(nc.sbuf_tensor(n, [P, NT], F32)) for n in names}

        psum = [est.enter_context(nc.psum_tensor(f"ps{i}", [P, 512], F32)) for i in range(8)]

        bld = B(nc, est)
        ld_x = bld.sem("ld_x"); ld_x2 = bld.sem("ld_x2"); ld_sh = bld.sem("ld_sh")
        ld_wk = bld.sem("ld_wk"); ld_wv = bld.sem("ld_wv"); ld_wq = bld.sem("ld_wq")
        ld_p = [bld.sem(f"ld_p{i}") for i in range(2)]
        pq = bld.sem("pq")
        ld_cg = [bld.sem(f"ld_cg{i}") for i in range(2)]
        ld_cu = [bld.sem(f"ld_cu{i}") for i in range(2)]
        ld_dj = [bld.sem(f"ld_dj{i}") for i in range(4)]
        ld_kv = bld.sem("ld_kv"); ld_x1r = bld.sem("ld_x1r"); ld_ws = bld.sem("ld_ws")
        st_kv = bld.sem("st_kv"); st_x1 = bld.sem("st_x1"); st_ws = bld.sem("st_ws")
        st_cd = bld.sem("st_cd"); st_cg = bld.sem("st_cg"); st_cu = bld.sem("st_cu")
        st_out = bld.sem("st_out")
        tr_ln = bld.sem("tr_ln"); tr_xq = bld.sem("tr_xq"); tr_hq = bld.sem("tr_hq")
        tr_p = bld.sem("tr_p")
        cc_ws = bld.sem("cc_ws"); cc_kv0 = bld.sem("cc_kv0"); cc_kv1 = bld.sem("cc_kv1")
        cc_g = bld.sem("cc_g"); cc_u = bld.sem("cc_u"); cc_d = bld.sem("cc_d")
        wsr = bld.sem("wsr")
        vch = bld.sem("vch"); sch = bld.sem("sch"); gch = bld.sem("gch")
        kvrdy = bld.sem("kvrdy")
        t_ms = bld.sem("t_ms")
        ph = bld.sem("ph")
        full = [bld.sem(f"full{i}") for i in range(8)]
        free = [bld.sem(f"free{i}") for i in range(8)]

        V, S, G, T, Y = "vector", "scalar", "gpsimd", "tensor", "sync"

        def ve(fn, waits=(), inc=None):
            bld.emit(V, fn, waits=waits, inc=(inc if inc is not None else vch))

        def se(fn, waits=(), inc=None):
            bld.emit(S, fn, waits=waits, inc=(inc if inc is not None else sch))

        def ge(fn, waits=(), inc=None):
            bld.emit(G, fn, waits=waits, inc=(inc if inc is not None else gch))

        def mm_group(bank, pairs, waits=(), out=None):
            o = out if out is not None else psum[bank][:]
            wl = bld.w(free[bank]) + tuple(waits)
            n = len(pairs)
            full[bank].n += 1

            def run(e, o=o, pairs=pairs, wl=wl, n=n, fh=full[bank].h):
                for (h, c) in wl:
                    e.wait_ge(h, c)
                for i, (l, r) in enumerate(pairs):
                    r_ = e.matmul(o, l, r, start=(i == 0), stop=(i == n - 1))
                    if i == n - 1:
                        r_.then_inc(fh, 1)
            bld.plan[T].append(run)

        att = ExitStack()
        qT = att.enter_context(nc.sbuf_tensor("qT", [P, 16384], BF16))
        wbfA = att.enter_context(nc.sbuf_tensor("wbfA", [P, 8192], BF16))
        wbfQ = att.enter_context(nc.sbuf_tensor("wbfQ", [P, 8192], BF16))
        wbfB = qT[:, 0:8192]
        kvev = att.enter_context(nc.sbuf_tensor("kvev", [P, 2048], F32))
        kvf32 = att.enter_context(nc.sbuf_tensor("kvf32", [P, 2048], F32))
        kvbf = wbfA[:, 0:8192]
        xld = att.enter_context(nc.sbuf_tensor("xld", [P, 4 * D], F32))
        lnbf = att.enter_context(nc.sbuf_tensor("lnbf", [P, D], BF16))
        x1t = att.enter_context(nc.sbuf_tensor("x1t", [P, D], F32))
        xqt = att.enter_context(nc.sbuf_tensor("xqt", [P, 2 * D], BF16))
        pstage = att.enter_context(nc.sbuf_tensor("pstage", [P, 2048], F32))
        pi32 = att.enter_context(nc.sbuf_tensor("pi32", [P, 2048], I32))
        pqbf = att.enter_context(nc.sbuf_tensor("pqbf", [P, 2048], BF16))
        ptrp = att.enter_context(nc.sbuf_tensor("ptrp", [P, 4096], BF16,
                                                is_transpose_dest=True))
        att2 = ExitStack()
        kbuf = att2.enter_context(nc.sbuf_tensor("kbuf", [P, 8 * D], BF16))
        vbuf = att2.enter_context(nc.sbuf_tensor("vbuf", [P, 8 * D], BF16))

        ve(lambda e: e.memset(ones_c[:], 1.0))
        ve(lambda e: e.memset(wsum_sb[:], 0.0))
        ve(lambda e: e.memset(ones_r[:], 1.0))
        ve(lambda e: e.memset(eps_c[:], EPS))

        def run_cc(e, i, o):
            return e.collective_compute(
                "AllReduce", op.add, replica_groups=[list(range(NCORE))],
                ins=[i], outs=[o])

        def ln_chain(sum_b, sq_b, r_b, nb_b, c0, c1):
            sl = slice(c0, c1)
            ve(lambda e: e.tensor_scalar(st["tmp1"][:, sl], sum_b[:, sl], INV_D, None,
                                         op.mult), waits=bld.w(vch, sch))
            bld.wait(V, vch)
            ve(lambda e: e.tensor_tensor(st["tmp2"][:, sl], st["tmp1"][:, sl],
                                         st["tmp1"][:, sl], op.mult))
            bld.wait(V, vch)
            ve(lambda e: e.scalar_tensor_tensor(st["tmp2"][:, sl], sq_b[:, sl], INV_D,
                                                st["tmp2"][:, sl], op.mult, op.subtract))
            se(lambda e: e.activation(r_b[:, sl], st["tmp2"][:, sl], AF.Sqrt,
                                      bias=eps_c[:, 0:1]), waits=bld.w(vch))
            ve(lambda e: e.reciprocal(r_b[:, sl], r_b[:, sl]), waits=bld.w(sch))
            bld.wait(V, vch)
            ve(lambda e: e.scalar_tensor_tensor(nb_b[:, sl], st["tmp1"][:, sl], -1.0,
                                                r_b[:, sl], op.mult, op.mult))

        def loadw_bf(w_h, dst_bf, ld_sem, waits=()):
            srcw = w_h.rearrange("(kb p) d -> p kb d", p=P)
            bld.dma(G, _r3(dst_bf, D), srcw, ld_sem, waits=waits)

        loadw_bf(wk_h, wbfA[:], ld_wk)
        loadw_bf(wv_h, wbfB, ld_wv)
        loadw_bf(wq_h, wbfQ[:], ld_wq)

        for grp in range(4):
            tiles = range(grp * 4, (grp + 1) * 4)
            src = x_h[grp * 512:(grp + 1) * 512, :].rearrange("(a p) d -> p a d", p=P)
            bld.dma(G, _r3(xld[:], D), src, ld_x, waits=bld.w(vch, sch) if grp else ())
            wl = ((ld_x.h, ld_x.n),)
            for i, t in enumerate(tiles):
                ve(lambda e, t=t, i=i: e.tensor_reduce(
                    st["a_sum"][:, t:t + 1], xld[:, i * D:(i + 1) * D], AX, op.add),
                    waits=wl)
                se(lambda e, t=t, i=i: e.activation(
                    x1t[:], xld[:, i * D:(i + 1) * D], AF.Square,
                    accum_out=st["a_sq"][:, t:t + 1]), waits=wl + bld.w(sch))
            ln_chain(st["a_sum"], st["a_sq"], st["a_r"], st["a_nb"], grp * 4, grp * 4 + 4)
            for i, t in enumerate(tiles):
                ve(lambda e, t=t, i=i: e.tensor_scalar(
                    lnbf[:], xld[:, i * D:(i + 1) * D],
                    st["a_r"][:, t:t + 1], st["a_nb"][:, t:t + 1], op.mult, op.add),
                    waits=bld.w(vch) + bld.w(tr_ln))
                dst = _r3(T1[:], 2048)[:, :, t * P:(t + 1) * P]
                srcv = _r3(lnbf[:], P)
                bld.dma(Y, dst, srcv, tr_ln, waits=bld.w(vch), transpose=True)

        for m in range(3):
            for hh in range(2):
                if m < 2:
                    sh_h = (wgsh_h, wush_h)[m]
                    ssrc = sh_h[hh * 256:(hh + 1) * 256, :].rearrange(
                        "(ob p) i -> p ob i", p=P)
                    sdst = _r3(kvf32[:], D)
                else:
                    ssrc = wdsh_h[hh * 512:(hh + 1) * 512, :].rearrange(
                        "(r p) j -> p r j", p=P)
                    sdst = _r3(kvf32[:], 512)
                bld.dma(G, sdst, ssrc, ld_sh, waits=bld.w(wsr))
                dstc = wsums[:, m:m + 1] if hh == 0 else st["tmp1"][:, 0:1]
                ve(lambda e, dstc=dstc: e.tensor_reduce(
                    dstc, kvf32[:], AX, op.add, apply_absolute_value=True),
                    waits=bld.w(ld_sh), inc=wsr)
            ve(lambda e, m=m: e.tensor_tensor(
                wsums[:, m:m + 1], wsums[:, m:m + 1], st["tmp1"][:, 0:1],
                op.add), waits=bld.w(wsr), inc=wsr)

        def wsum_b1():
            mm_group(7, [(ones_c[:], wsums[:, 0:3])], waits=bld.w(wsr),
                     out=psum[7][0:1, 0:3])
            ve(lambda e: e.tensor_copy(wsum_sb[0:1, 0:3], psum[7][0:1, 0:3]),
               waits=bld.w(full[7]), inc=free[7])
            bld.dma(G, wsum_in, wsum_sb[:], st_ws, waits=bld.w(free[7]))
            bld.emit(G, lambda e: run_cc(e, wsum_in, wsum_out),
                     waits=bld.w(st_ws), inc=cc_ws)
            bld.dma(G, wpack[:], wsum_out, ld_ws, waits=bld.w(cc_ws))
            ve(lambda e: e.tensor_scalar(wpack[0:1, 0:3], wpack[0:1, 0:3], INV_NW,
                                         1e-5, op.mult, op.max),
               waits=bld.w(ld_ws), inc=wsr)
            ve(lambda e: e.tensor_scalar(wpack[0:1, 3:6], wpack[0:1, 0:3], INV127,
                                         None, op.mult), waits=bld.w(wsr), inc=wsr)
            ve(lambda e: e.reciprocal(wpack[0:1, 0:3], wpack[0:1, 0:3]),
               waits=bld.w(wsr), inc=wsr)

        def wsum_b2():
            mm_group(7, [(ones_r[:], wpack[0:1, 0:8])], waits=bld.w(wsr),
                     out=psum[7][:, 0:8])
            ve(lambda e: e.tensor_copy(wstats[:], psum[7][:, 0:8]),
               waits=bld.w(full[7]), inc=free[7])
            ws_rdy[0] = bld.w(free[7])

        ws_rdy = [()]

        psub = [0]
        last_pst = [()]

        def _prep_sub(load_src, scol, dst, store_fn=None, width=1024):
            n = psub[0]
            psub[0] += 1
            slot = n % 2
            bld.dma(G, pstage[:, slot * 1024: slot * 1024 + width], load_src,
                    ld_p[slot], waits=((pq.h, n - 1),) if n >= 2 else ())
            ve(lambda e, slot=slot, width=width: e.tensor_scalar(
                pi32[:, 0:width], pstage[:, slot * 1024: slot * 1024 + width],
                wstats[:, scol:scol + 1], None, op.mult),
                waits=bld.w(ld_p[slot]) + tuple(ws_rdy[0]), inc=pq)
            qs = n % 2
            bld.wait(V, pq)
            ve(lambda e, qs=qs, width=width: e.tensor_scalar(
                pqbf[:, qs * 1024: qs * 1024 + width], pi32[:, 0:width], 1.0, -1.0,
                op.min, op.max), waits=((tr_p.h, max(0, 16 * (psub[0] - 2))),))
            srcv = _r3(pqbf[:, qs * 1024: qs * 1024 + width], P)
            bld.dma(Y, dst, srcv, tr_p,
                    waits=bld.w(vch) + tuple(last_pst[0]), transpose=True)
            if store_fn is not None:
                store_fn()

        def _mk_store(cache_slice, src_view, st_c):
            def store():
                bld.dma(Y, cache_slice, src_view, st_c, waits=bld.w(tr_p))
                last_pst[0] = ((st_c.h, st_c.n),)
            return store

        prep_units = []
        for r in range(8):
            load_src = wdsh_h[r * P:(r + 1) * P, :]
            dst = _r3(ptrp[:], 1024)[:, :, r * P:(r + 1) * P]
            sf = None
            if r == 7:
                csl = cqd_in.rearrange("(q p) d -> p q d", p=P)
                sf = _mk_store(csl, _r3(ptrp[:], 1024), st_cd)
            prep_units.append((load_src, 2, dst, sf, 512))
        for m_h, scol, cache_in, st_c in ((wgsh_h, 0, cqg_in, st_cg),
                                          (wush_h, 1, cqu_in, st_cu)):
            for ob in range(4):
                load_src = m_h[ob * P:(ob + 1) * P, :]
                dst = _r3(ptrp[:], 512)[:, :, ob * P:(ob + 1) * P]
                sf = None
                if ob == 3:
                    csl = cache_in.rearrange("(ib p) o -> p ib o", p=P)
                    sf = _mk_store(csl, _r3(ptrp[:], 512), st_c)
                prep_units.append((load_src, scol, dst, sf, 1024))

        prep_i = [0]

        def emit_prep(n=1):
            if not ws_rdy[0]:
                return
            for _ in range(n):
                if prep_i[0] < len(prep_units):
                    prep_units[prep_i[0]]
                    _prep_sub(*prep_units[prep_i[0]])
                    prep_i[0] += 1



        def kv_proj(b, wbf, obuf, wsem):
            gi = 0
            for tb in range(TB):
                tglob = b * TB + tb
                for oc in range(2):
                    bank = gi % 4
                    gi += 1
                    pairs = []
                    for kb in range(8):
                        l = T1[:, kb * 2048 + tglob * P: kb * 2048 + (tglob + 1) * P]
                        r = wbf[:, kb * D + oc * 512: kb * D + (oc + 1) * 512]
                        pairs.append((l, r))
                    mm_group(bank, pairs,
                             waits=((tr_ln.h, 16 * (tglob + 1)),) + bld.w(wsem))
                    se(lambda e, bank=bank, obuf=obuf, tb=tb, oc=oc: e.copy(
                        obuf[:, tb * D + oc * 512: tb * D + (oc + 1) * 512],
                        psum[bank][:]), waits=bld.w(full[bank]), inc=free[bank])
                    emit_prep(1)

        def kv_partial(b):
            for jc in range(2):
                for g in range(2):
                    banks = [4 * g + ii for ii in range(4)]
                    for ii in range(4):
                        ib = g * 4 + ii
                        pairs = []
                        for tb in range(TB):
                            l = kbuf[:, tb * D + ib * P: tb * D + (ib + 1) * P]
                            r = vbuf[:, tb * D + jc * 512: tb * D + (jc + 1) * 512]
                            pairs.append((l, r))
                        mm_group(banks[ii], pairs, waits=bld.w(*free[0:4]))
                    for ii in range(4):
                        bank = banks[ii]
                        ve(lambda e, bank=bank, ii=ii: e.tensor_copy(
                            kvev[:, ii * 512:(ii + 1) * 512], psum[bank][:]),
                            waits=bld.w(full[bank]) + bld.w(st_kv), inc=free[bank])
                    dst = kvin[b].rearrange("(ib p) j -> p ib j", p=P)[
                        :, g * 4:(g + 1) * 4, jc * 512:(jc + 1) * 512]
                    bld.dma(Y, dst, _r3(kvev[:], 512), st_kv,
                            waits=bld.w(*free[4 * g:4 * g + 4]))
                    emit_prep(1)

        kv_proj(0, wbfA[:], kbuf[:], ld_wk)
        wsum_b1()
        kv_proj(0, wbfB, vbuf[:], ld_wv)
        wsum_b2()
        kv_partial(0)
        bld.emit(G, lambda e: run_cc(e, kvin[0], kvout[0]), waits=bld.w(st_kv),
                 inc=cc_kv0)

        kv_proj(1, wbfA[:], kbuf[:], ld_wk)
        kv_proj(1, wbfB, vbuf[:], ld_wv)
        kv_partial(1)
        bld.emit(G, lambda e: run_cc(e, kvin[1], kvout[1]), waits=bld.w(st_kv),
                 inc=cc_kv1)

        bld.nop_inc(T, t_ms, 1)
        ms_kv_done = t_ms.n
        for dqb in range(8):
            for tc in range(4):
                bank = (dqb * 4 + tc) % 4
                pairs = []
                for kb in range(8):
                    l = wbfQ[:, kb * D + dqb * P: kb * D + (dqb + 1) * P]
                    r = T1[:, kb * 2048 + tc * 512: kb * 2048 + (tc + 1) * 512]
                    pairs.append((l, r))
                mm_group(bank, pairs, waits=bld.w(ld_wq))
                se(lambda e, bank=bank, dqb=dqb, tc=tc: e.copy(
                    qT[:, dqb * 2048 + tc * 512: dqb * 2048 + (tc + 1) * 512],
                    psum[bank][:]), waits=bld.w(full[bank]), inc=free[bank])
                emit_prep(1)
        bld.nop_inc(T, t_ms, 1)
        ms_q_done = t_ms.n

        emit_prep(99)

        def run_ag(e, i, o):
            return e.collective_compute(
                "AllGather", op.bypass, replica_groups=[list(range(NCORE))],
                ins=[i], outs=[o])
        bld.emit(G, lambda e: run_ag(e, cqd_in, cqd), waits=bld.w(st_cd), inc=cc_d)
        bld.emit(G, lambda e: run_ag(e, cqg_in, cqg), waits=bld.w(st_cg), inc=cc_g)
        bld.emit(G, lambda e: run_ag(e, cqu_in, cqu), waits=bld.w(st_cu), inc=cc_u)

        att2.close()
        ybf = att.enter_context(nc.sbuf_tensor("ybf", [P, NT * D], BF16))

        for b in range(2):
            ccb = cc_kv0 if b == 0 else cc_kv1
            for h in range(4):
                src = kvout[b][h * 256:(h + 1) * 256, :].rearrange("(ib p) j -> p ib j",
                                                                   p=P)
                bld.dma(S, _r3(kvf32[:], D), src, ld_kv,
                        waits=((ccb.h, 1),) + bld.w(kvrdy) +
                              (bld.w(vch) if (b == 0 and h == 0) else ()))
                wkv = bld.w(ld_kv) + ((t_ms.h, (ms_kv_done if b == 0 else t_ms.n)),)
                ve(lambda e, h=h: e.tensor_scalar(
                    kvbf[:, h * 2048:(h + 1) * 2048], kvf32[:], GAMMA, None, op.mult),
                    waits=wkv, inc=kvrdy)

            for tb in range(TB):
                t = b * TB + tb
                for jc in range(2):
                    bank = (tb * 2 + jc) % 4
                    pairs = []
                    for dmb in range(8):
                        l = qT[:, dmb * 2048 + t * P: dmb * 2048 + (t + 1) * P]
                        r = kvbf[:, dmb * D + jc * 512: dmb * D + (jc + 1) * 512]
                        pairs.append((l, r))
                    mm_group(bank, pairs, waits=bld.w(kvrdy))
                    ve(lambda e, bank=bank, t=t, jc=jc: e.tensor_copy(
                        ybf[:, t * D + jc * 512: t * D + (jc + 1) * 512],
                        psum[bank][:]), waits=bld.w(full[bank]), inc=free[bank])
            bld.nop_inc(T, t_ms, 1)

        x1s = [kvf32[:, 0:1024], kvev[:, 0:1024]]
        ln2s = [kvf32[:, 1024:2048], kvev[:, 1024:2048]]
        qis = [_r3(pi32[:], 1024)[:, 0, :], _r3(pi32[:], 1024)[:, 1, :]]
        KVEV_FREE = bld.w(st_kv)
        KVF_FREE = bld.w(kvrdy)
        for grp in range(4):
            tiles = list(range(grp * 4, (grp + 1) * 4))
            for i, t in enumerate(tiles):
                wly = bld.w(*free[0:4])
                ve(lambda e, t=t: e.tensor_reduce(
                    st["y_sum"][:, t:t + 1], ybf[:, t * D:(t + 1) * D], AX, op.add),
                    waits=wly)
                se(lambda e, t=t: e.activation(
                    x1t[:], ybf[:, t * D:(t + 1) * D], AF.Square,
                    accum_out=st["y_sq"][:, t:t + 1]), waits=wly + bld.w(sch))
            ln_chain(st["y_sum"], st["y_sq"], st["y_r"], st["y_nb"], grp * 4, grp * 4 + 4)
            src = x_h[grp * 512:(grp + 1) * 512, :].rearrange("(a p) d -> p a d", p=P)
            bld.dma(G, _r3(xld[:], D), src, ld_x2, waits=bld.w(vch))
            for pr in range(2):
                pts = tiles[pr * 2:(pr + 1) * 2]
                for s, t in enumerate(pts):
                    i = pr * 2 + s
                    bld.wait(V, vch)
                    ve(lambda e, t=t, s=s: e.tensor_scalar(
                        x1s[s], ybf[:, t * D:(t + 1) * D], st["y_r"][:, t:t + 1],
                        st["y_nb"][:, t:t + 1], op.mult, op.add),
                        waits=bld.w(st_x1) +
                              (KVEV_FREE if s else KVF_FREE))
                    bld.wait(V, vch)
                    ve(lambda e, s=s, i=i: e.tensor_tensor(
                        x1s[s], x1s[s], xld[:, i * D:(i + 1) * D], op.add),
                        waits=bld.w(ld_x2))
                for s, t in enumerate(pts):
                    bld.dma(Y, x1sp[t * P:(t + 1) * P, :], x1s[s], st_x1,
                            waits=bld.w(vch))
                for s, t in enumerate(pts):
                    bld.wait(V, vch)
                    ve(lambda e, t=t, s=s: e.tensor_reduce(
                        st["b_sum"][:, t:t + 1], x1s[s], AX, op.add))
                    se(lambda e, t=t, s=s: e.activation(
                        ybf[:, t * D:(t + 1) * D], x1s[s], AF.Square,
                        accum_out=st["b_sq"][:, t:t + 1]), waits=bld.w(vch, sch))
                ln_chain(st["b_sum"], st["b_sq"], st["b_r"], st["b_nb"],
                         pts[0], pts[0] + 2)
                for s, t in enumerate(pts):
                    bld.wait(V, vch)
                    ve(lambda e, t=t, s=s: e.tensor_scalar(
                        ln2s[s], x1s[s], st["b_r"][:, t:t + 1],
                        st["b_nb"][:, t:t + 1], op.mult, op.add),
                        waits=bld.w(sch))
                for s, t in enumerate(pts):
                    bld.wait(V, vch)
                    ve(lambda e, t=t, s=s: e.tensor_reduce(
                        st["mx"][:, t:t + 1], ln2s[s], AX, op.max,
                        apply_absolute_value=True))
                bld.wait(V, vch)
                ve(lambda e, t0=pts[0]: e.tensor_scalar(
                    st["clipx"][:, t0:t0 + 2], st["mx"][:, t0:t0 + 2], 1e-5, None,
                    op.max))
                bld.wait(V, vch)
                ve(lambda e, t0=pts[0]: e.reciprocal(st["tmp2"][:, 0:2],
                                                     st["clipx"][:, t0:t0 + 2]))
                bld.wait(V, vch)
                ve(lambda e, t0=pts[0]: e.tensor_scalar(
                    st["sx"][:, t0:t0 + 2], st["tmp2"][:, 0:2], 127.0, None, op.mult))
                for s, t in enumerate(pts):
                    bld.wait(V, vch)
                    ve(lambda e, t=t, s=s: e.tensor_scalar(
                        qis[s], ln2s[s], st["sx"][:, t:t + 1], None, op.mult))
                    xslot = t % 2
                    se(lambda e, s=s, xs=xslot: e.copy(
                        xqt[:, xs * D:(xs + 1) * D], qis[s]),
                        waits=bld.w(vch) + bld.w(tr_xq))
                    dst = _r3(T1[:], 2048)[:, :, t * P:(t + 1) * P]
                    srcv = _r3(xqt[:, xslot * D:(xslot + 1) * D], P)
                    bld.dma(Y, dst, srcv, tr_xq,
                            waits=bld.w(sch, vch) + ((t_ms.h, ms_q_done),),
                            transpose=True)

        ve(lambda e: e.tensor_scalar(st["dscg"][:], st["clipx"][:], wstats[:, 3:4],
                                     None, op.mult), waits=bld.w(vch))
        bld.wait(V, vch)
        ve(lambda e: e.tensor_scalar(st["dscu"][:], st["clipx"][:], wstats[:, 4:5],
                                     None, op.mult))

        bld.wait(Y, tr_ln, st_kv, st_x1, tr_xq, tr_p, st_cd, st_cg, st_cu)
        bld.wait(G, ld_x, ld_sh, ld_wk, ld_wv, ld_wq, ld_ws, *ld_p)
        bld.emit(V, lambda e: e.memset(junkV[:], 0.0), inc=ph)
        bld.emit(S, lambda e: e.memzero(junkS[:]), inc=ph)
        bld.emit(G, lambda e: e.memset(junkG[:], 0.0), inc=ph)
        bld.nop_inc(T, ph, 1)
        bld.nop_inc(Y, ph, 1)
        att.close()

        ffn = ExitStack()
        hbuf = ffn.enter_context(nc.sbuf_tensor("hbuf", [P, 8 * F], BF16))
        hqTg = ffn.enter_context(nc.sbuf_tensor("hqTg", [P, 32 * 512], BF16,
                                                is_transpose_dest=True))
        wqtg = ffn.enter_context(nc.sbuf_tensor("wqtg", [P, 2 * 4096], BF16,
                                                is_transpose_dest=True))
        wqtu = ffn.enter_context(nc.sbuf_tensor("wqtu", [P, 2 * 4096], BF16,
                                                is_transpose_dest=True))
        wdqb = ffn.enter_context(nc.sbuf_tensor("wdqb", [P, 4 * D], BF16))
        hqi32 = ffn.enter_context(nc.sbuf_tensor("hqi32", [P, 2048], I32))
        hqbf = ffn.enter_context(nc.sbuf_tensor("hqbf", [P, F], BF16,
                                                is_transpose_dest=True))
        gsil = ffn.enter_context(nc.sbuf_tensor("gsil", [P, 2 * 512], F32))
        dnf = ffn.enter_context(nc.sbuf_tensor("dnf", [P, D], F32))
        x1r = ffn.enter_context(nc.sbuf_tensor("x1r", [P, D], F32))
        ftrash = ffn.enter_context(nc.sbuf_tensor("ftrash", [P, D], F32))

        for eng in (V, S, G, T, Y):
            bld.wait(eng, ph)

        def cache_load(cache_h, sems, dstbuf, oc, extra=()):
            slot = oc % 2
            src = cache_h[oc * D:(oc + 1) * D, :].rearrange("(ib p) o -> p ib o", p=P)
            bld.dma(Y, _r3(dstbuf[:, slot * 4096:(slot + 1) * 4096], 512), src,
                    sems[slot], waits=tuple(extra))

        h_rdy = {}
        prev_down = [()]
        for half in range(2):
            tiles = list(range(half * TB, (half + 1) * TB))
            for oc in range(8):
                if half == 0 and oc < 2:
                    cache_load(cqg, ld_cg, wqtg[:], oc, extra=((cc_g.h, 1),))
                    cache_load(cqu, ld_cu, wqtu[:], oc, extra=((cc_u.h, 1),))
                slot = oc % 2
                wts = ((ld_cg[slot].h, ld_cg[slot].n), (ld_cu[slot].h, ld_cu[slot].n),
                       (tr_xq.h, 16 * NT))
                for ti, t in enumerate(tiles):
                    gbank = 2 * (ti % 2)
                    ubank = gbank + 1
                    gp, up = [], []
                    for ib in range(8):
                        l = T1[:, ib * 2048 + t * P: ib * 2048 + (t + 1) * P]
                        gp.append((l, wqtg[:, slot * 4096 + ib * 512:
                                            slot * 4096 + (ib + 1) * 512]))
                        up.append((l, wqtu[:, slot * 4096 + ib * 512:
                                            slot * 4096 + (ib + 1) * 512]))
                    mm_group(gbank, gp, waits=wts)
                    mm_group(ubank, up)
                    gslot = ti % 2
                    se(lambda e, gbank=gbank, t=t, s=gslot: e.activation(
                        gsil[:, s * 512:(s + 1) * 512], psum[gbank][:], AF.Silu,
                        scale=st["dscg"][:, t:t + 1]),
                        waits=bld.w(full[gbank]) + bld.w(*free[0:4]),
                        inc=free[gbank])
                    hcol = (t % TB) * F + oc * 512
                    ve(lambda e, ubank=ubank, t=t, s=gslot, hcol=hcol:
                        e.scalar_tensor_tensor(
                            hbuf[:, hcol:hcol + 512], psum[ubank][:],
                            st["dscu"][:, t:t + 1], gsil[:, s * 512:(s + 1) * 512],
                            op.mult, op.mult),
                        waits=bld.w(full[ubank], free[gbank]), inc=free[ubank])
                    if oc == 7:
                        h_rdy[t] = (free[ubank].h, free[ubank].n)
                bld.nop_inc(T, t_ms, 1)
                nh, noc = (half, oc + 2) if oc + 2 < 8 else (half + 1, oc + 2 - 8)
                if nh < 2:
                    gate_w = ((t_ms.h, t_ms.n),)
                    cache_load(cqg, ld_cg, wqtg[:], noc, extra=gate_w)
                    cache_load(cqu, ld_cu, wqtu[:], noc, extra=gate_w)

            for grp in range(2):
                gtiles = tiles[grp * 4:(grp + 1) * 4]
                pdw = tuple(prev_down[0])
                for gi, t in enumerate(gtiles):
                    tl = t % TB
                    bld.wait(V, vch)
                    ve(lambda e, tl=tl: e.tensor_reduce(
                        st["mx"][:, 0:1], hbuf[:, tl * F:(tl + 1) * F], AX, op.max,
                        apply_absolute_value=True), waits=(h_rdy[t],))
                    bld.wait(V, vch)
                    ve(lambda e, t=t: e.tensor_scalar(st["cliph"][:, t:t + 1],
                                                      st["mx"][:, 0:1], 1e-5, None,
                                                      op.max))
                    bld.wait(V, vch)
                    ve(lambda e, t=t: e.reciprocal(st["tmp2"][:, 0:1],
                                                   st["cliph"][:, t:t + 1]))
                    bld.wait(V, vch)
                    ve(lambda e, t=t: e.tensor_scalar(st["sh"][:, t:t + 1],
                                                      st["tmp2"][:, 0:1], 127.0,
                                                      None, op.mult))
                    bld.wait(V, vch)
                    ve(lambda e, t=t: e.tensor_scalar(st["dscd"][:, t:t + 1],
                                                      st["cliph"][:, t:t + 1],
                                                      wstats[:, 5:6], None, op.mult))
                    for hh in range(2):
                        bld.wait(V, vch, sch)
                        ve(lambda e, tl=tl, t=t, hh=hh: e.tensor_scalar(
                            hqi32[:],
                            hbuf[:, tl * F + hh * 2048: tl * F + (hh + 1) * 2048],
                            st["sh"][:, t:t + 1], None, op.mult))
                        se(lambda e, hh=hh: e.copy(
                            hqbf[:, hh * 2048:(hh + 1) * 2048], hqi32[:]),
                           waits=bld.w(vch) + bld.w(tr_hq))
                    dst = _r3(hqTg[:], 512)[:, :, gi * P:(gi + 1) * P]
                    bld.dma(Y, dst, _r3(hqbf[:], P), tr_hq,
                            waits=bld.w(sch, vch) + pdw, transpose=True)
                for bank in range(8):
                    bld.wait(T, free[bank])
                t_ms_base = t_ms.n
                for jb in range(32):
                    slot = jb % 4
                    ldw = (((t_ms.h, t_ms_base + jb - 3),) if jb >= 4
                           else ((cc_d.h, 1),) + pdw)
                    bld.dma(Y, wdqb[:, slot * D:(slot + 1) * D],
                            cqd[jb * P:(jb + 1) * P, :], ld_dj[slot], waits=ldw)
                    wl = ((ld_dj[slot].h, ld_dj[slot].n), (tr_hq.h, tr_hq.n))
                    last = (jb == 31)
                    if not last:
                        t_ms.n += 1
                    else:
                        for bank in range(8):
                            full[bank].n += 1

                    def runj(e, jb=jb, slot=slot, wl=wl, last=last,
                             tmh=t_ms.h, fhs=[full[i].h for i in range(8)]):
                        for (h, n) in wl:
                            e.wait_ge(h, n)
                        k = 0
                        for tg in range(4):
                            for oc2 in range(2):
                                r_ = e.matmul(
                                    psum[tg * 2 + oc2][:],
                                    hqTg[:, jb * 512 + tg * P: jb * 512 + (tg + 1) * P],
                                    wdqb[:, slot * D + oc2 * 512:
                                         slot * D + (oc2 + 1) * 512],
                                    start=(jb == 0), stop=last)
                                if last:
                                    r_.then_inc(fhs[tg * 2 + oc2], 1)
                                elif k == 7:
                                    r_.then_inc(tmh, 1)
                                k += 1
                    bld.plan[T].append(runj)
                prev_down[0] = bld.w(full[7])
                for gi, t in enumerate(gtiles):
                    se(lambda e, gi=gi, t=t: e.activation(
                        dnf[:, 0:512], psum[gi * 2][:], AF.Copy,
                        scale=st["dscd"][:, t:t + 1]),
                        waits=bld.w(full[gi * 2], vch) + bld.w(st_out),
                        inc=free[gi * 2])
                    se(lambda e, gi=gi, t=t: e.activation(
                        dnf[:, 512:1024], psum[gi * 2 + 1][:], AF.Copy,
                        scale=st["dscd"][:, t:t + 1]),
                        waits=bld.w(full[gi * 2 + 1]), inc=free[gi * 2 + 1])
                    ve(lambda e, t=t: e.tensor_reduce(st["d_sum"][:, t:t + 1], dnf[:],
                                                      AX, op.add),
                       waits=bld.w(free[gi * 2], free[gi * 2 + 1]))
                    se(lambda e, t=t: e.activation(ftrash[:], dnf[:], AF.Square,
                                                   accum_out=st["d_sq"][:, t:t + 1]),
                       waits=bld.w(free[gi * 2], free[gi * 2 + 1], sch))
                    ln_chain(st["d_sum"], st["d_sq"], st["d_r"], st["d_nb"], t, t + 1)
                    bld.wait(V, vch)
                    ve(lambda e, t=t: e.tensor_scalar(dnf[:], dnf[:],
                                                      st["d_r"][:, t:t + 1],
                                                      st["d_nb"][:, t:t + 1],
                                                      op.mult, op.add))
                    bld.dma(G, x1r[:], x1sp[t * P:(t + 1) * P, :], ld_x1r,
                            waits=bld.w(vch) + bld.w(st_x1))
                    bld.wait(V, vch)
                    ve(lambda e, t=t: e.tensor_tensor(dnf[:], x1r[:], dnf[:], op.add),
                       waits=bld.w(ld_x1r, sch))
                    bld.dma(Y, out_h[t * P:(t + 1) * P, :], dnf[:], st_out,
                            waits=bld.w(vch, sch))

        bld.wait(Y, st_out)
        ffn.close()

        with nc.Block() as block:
            @block.sync
            def _(e):
                for f in bld.plan["sync"]:
                    f(e)

            @block.tensor
            def _(e):
                for f in bld.plan["tensor"]:
                    f(e)

            @block.vector
            def _(e):
                for f in bld.plan["vector"]:
                    f(e)

            @block.scalar
            def _(e):
                for f in bld.plan["scalar"]:
                    f(e)

            @block.gpsimd
            def _(e):
                for f in bld.plan["gpsimd"]:
                    f(e)

    return nc


_NC = None


def _get_nc():
    global _NC
    if _NC is None:
        _NC = build_nc()
    return _NC


def kernel(L0_attn_x, L0_attn_w_q, L0_attn_w_k, L0_attn_w_v,
           L0_ffn_w_gate, L0_ffn_w_up, L0_ffn_w_down):
    x = np.ascontiguousarray(np.asarray(L0_attn_x, dtype=np.float32))
    wq = np.ascontiguousarray(np.asarray(L0_attn_w_q, dtype=np.float32))
    wk = np.ascontiguousarray(np.asarray(L0_attn_w_k, dtype=np.float32))
    wv = np.ascontiguousarray(np.asarray(L0_attn_w_v, dtype=np.float32))
    wg = np.ascontiguousarray(np.asarray(L0_ffn_w_gate, dtype=np.float32))
    wu = np.ascontiguousarray(np.asarray(L0_ffn_w_up, dtype=np.float32))
    wd = np.ascontiguousarray(np.asarray(L0_ffn_w_down, dtype=np.float32))

    nc = _get_nc()
    in_maps = []
    for c in range(NCORE):
        sl = slice(c * 1024, (c + 1) * 1024)
        xc = np.ascontiguousarray(np.concatenate([x[0, sl], x[1, sl]], axis=0))
        in_maps.append({
            "x": xc, "wq": wq, "wk": wk, "wv": wv, "wg": wg, "wu": wu, "wd": wd,
            "wgsh": np.ascontiguousarray(wg[c * 512:(c + 1) * 512]),
            "wush": np.ascontiguousarray(wu[c * 512:(c + 1) * 512]),
            "wdsh": np.ascontiguousarray(wd[:, c * 512:(c + 1) * 512]),
        })
    res = run_bass_kernel_spmd(nc, in_maps, core_ids=list(range(NCORE)))
    out = np.empty((2, 8192, 1024), np.float32)
    for c in range(NCORE):
        sl = slice(c * 1024, (c + 1) * 1024)
        r = np.asarray(res.results[c]["out"], np.float32)
        out[0, sl] = r[:1024]
        out[1, sl] = r[1024:]
    return out
